# version 08 (final)
# baseline (speedup 1.0000x reference)
import numpy as np
import concourse.bass as bass
import concourse.mybir as mybir
from concourse.bass_utils import run_bass_kernel_spmd
from concourse.alu_op_type import AluOpType as OP

F32 = mybir.dt.float32
BF16 = mybir.dt.bfloat16
AF = mybir.ActivationFunctionType
AX = mybir.AxisListType

D = 4096
TO = 2048
NCH = 16
NH = 128
NG = 8
INTER = 8192
CONVD = 10240
PROJ = 18560
NDB = 32
TB = 512
NEG = -1.0e9
EPS = 1e-5
RG = [[0, 1], [2, 3], [4, 5], [6, 7]]
DEBUG = [False]
SAFE = [True]


def build_program():
    nc = bass.Bass("TRN2", target_bir_lowering=False, debug=False)

    def din(name, shape, dt=F32):
        return nc.dram_tensor(name, list(shape), dt, kind="ExternalInput").ap()

    def dscr(name, shape, dt=F32):
        kind = "ExternalOutput" if (DEBUG[0] and not name.startswith("cc")) else "Internal"
        return nc.dram_tensor(name, list(shape), dt, kind=kind).ap()

    xT = din("xT", [D, TO])
    w_upT = din("w_upT", [D, D])
    gwa_T = din("gwa_T", [D, 36])
    eb_row = din("eb_row", [1, 4])
    b_all = din("b_all", [32, D])
    w_inT = din("w_inT", [D, PROJ])
    convw = din("convw", [128, 320])
    convb = din("convb", [128, 80])
    dtb_rep = din("dtb_rep", [128, 128])
    negA_rep = din("negA_rep", [128, 128])
    Drep = din("Drep", [128, INTER])
    normw_rep = din("normw_rep", [128, INTER])
    w_outT = din("w_outT", [INTER, D])
    LT1 = din("LT1", [128, 128])
    trilneg4 = din("trilneg4", [128, 512])
    idn = din("idn", [128, 128])
    m_send = din("m_send", [128, 1])
    m_recv = din("m_recv", [128, 1])

    xoutT = nc.dram_tensor("xoutT", [D, TO], F32, kind="ExternalOutput").ap()

    x1T_d = dscr("x1T_d", [D, TO])
    h2T_d = dscr("h2T_d", [D, TO], BF16)
    gateT_d = dscr("gateT_d", [INTER, TO], BF16)
    xBCT_d = dscr("xBCT_d", [CONVD, TO], BF16)
    dtF_d = dscr("dtF_d", [128, TO])
    xstok_d = dscr("xstok_d", [NCH * 128, INTER], BF16)
    btok_d = dscr("btok_d", [NCH * 128, NG * 128], BF16)
    bF_d = dscr("bF_d", [NG * 128, TO], BF16)
    cF_d = dscr("cF_d", [NG * 128, TO], BF16)
    ynT_d = dscr("ynT_d", [INTER, TO], BF16)
    x2T_d = dscr("x2T_d", [D, TO])
    wupbf_d = dscr("wupbf_d", [D, D], BF16)
    csf_d = dscr("csf_d", [128, 128])
    cc1i = dscr("cc1i", [128, 240], BF16)
    cc1o = dscr("cc1o", [128, 240], BF16)
    cc2i = dscr("cc2i", [128, INTER])
    cc2o = dscr("cc2o", [128, INTER])
    dbg_cs = dscr("dbg_cs", [128, NCH * 128])
    dbg_lndt = dscr("dbg_lndt", [128, NCH * 128])
    dbg_cd = dscr("dbg_cd", [1, NCH * 128])
    dbg_prevT = dscr("dbg_prevT", [128, INTER])
    dbg_y = dscr("dbg_y", [128, INTER], BF16)
    dbg_yg = dscr("dbg_yg", [128, INTER], BF16)
    dbg_gnst = dscr("dbg_gnst", [128, 24])

    open_mgrs = []

    class Scope:
        def __init__(self):
            self.mgrs = []

        def sb(self, name, shape, dt):
            m = nc.sbuf_tensor(name, list(shape), dt)
            self.mgrs.append(m)
            return m.__enter__()

        def ps(self, name, shape, dt=F32):
            m = nc.psum_tensor(name, list(shape), dt)
            self.mgrs.append(m)
            return m.__enter__()

        def close(self):
            for m in reversed(self.mgrs):
                m.__exit__(None, None, None)

    gsc = Scope()
    ones_colf = gsc.sb("ones_colf", [128, 1], F32)
    ones_rowf = gsc.sb("ones_rowf", [1, 128], F32)
    ones_rowb = gsc.sb("ones_rowb", [1, 128], BF16)
    idn_f = gsc.sb("idn_f", [128, 128], F32)
    idn_b = gsc.sb("idn_b", [128, 128], BF16)
    LT1_s = gsc.sb("LT1_s", [128, 128], F32)
    tril4_s = gsc.sb("tril4_s", [128, 512], F32)
    convw_s = gsc.sb("convw_s", [128, 320], BF16)
    convb_s = gsc.sb("convb_s", [128, 80], BF16)
    dtb_s = gsc.sb("dtb_s", [128, 128], F32)
    negA_s = gsc.sb("negA_s", [128, 128], F32)
    msend_s = gsc.sb("msend_s", [128, 1], F32)
    mrecv_s = gsc.sb("mrecv_s", [128, 1], F32)
    eb_s = gsc.sb("eb_s", [1, 4], BF16)
    ball_s = gsc.sb("ball_s", [32, D], BF16)
    gwa_s = gsc.sb("gwa_s", [128, NDB * 36], BF16)
    CONSTS = [EPS, 1024.0 * EPS, -0.5, 1.0 / D, -1.0, 0.5, 100.0, 0.001, 1.0, 0.0, 65536.0, -65536.0]
    cpool = gsc.sb("cpool", [128, len(CONSTS)], F32)
    for _i, _v in enumerate(CONSTS):
        try:
            nc.const_aps.register(F32, _v, cpool[:, _i:_i + 1])
        except Exception:
            pass

    sems, cnt = {}, {}
    for k in ("T", "V", "S", "G", "DS", "DG"):
        m = nc.semaphore("sem" + k)
        open_mgrs.append(m)
        sems[k] = m.__enter__()
        cnt[k] = 0

    blk_mgr = nc.Block()
    block = blk_mgr.__enter__()

    prog = []

    def all_deps(*exclude):
        return {k: cnt[k] for k in cnt if k not in exclude and cnt[k] > 0}

    def step(engkey, fns, deps="all", dinc=None):
        if deps == "all" or (SAFE[0] and deps == "all"):
            deps = all_deps(engkey)
        elif deps == "all":
            deps = all_deps(engkey)
        else:
            extra = all_deps(engkey) if SAFE[0] else {}
            d2 = dict(extra)
            for k, v in dict(deps).items():
                if v > 0:
                    d2[k] = max(d2.get(k, 0), v)
            deps = d2
        if dinc is None:
            dinc = 16 if engkey in ("DS", "DG") else 1
        fns = list(fns)
        assert fns
        semk = sems[engkey]
        dsems = [(sems[k], v) for k, v in deps.items()]

        isdma = engkey in ("DS", "DG")

        def run(eng):
            for s_, v_ in dsems:
                eng.wait_ge(s_, v_)
            last = None
            for f in fns:
                last = f(eng)
                if isdma:
                    last.then_inc(semk, 16)
            if not isdma:
                last.then_inc(semk, dinc)
        prog.append((engkey, run))
        cnt[engkey] += 16 * len(fns) if isdma else dinc
        return cnt[engkey]

    step("DG", [
        lambda g: g.dma_start(out=idn_b[:], in_=idn),
        lambda g: g.dma_start(out=eb_s[:], in_=eb_row),
        lambda g: g.dma_start(out=ball_s[:], in_=b_all),
        lambda g: g.dma_start(out=convw_s[:], in_=convw),
        lambda g: g.dma_start(out=convb_s[:], in_=convb),
        lambda g: g.dma_start(out=gwa_s[:].rearrange("p (k n) -> p k n", n=36),
                              in_=gwa_T.rearrange("(k p) n -> p k n", p=128)),
    ], deps={})
    step("DS", [
        lambda s: s.dma_start(out=idn_f[:], in_=idn),
        lambda s: s.dma_start(out=LT1_s[:], in_=LT1),
        lambda s: s.dma_start(out=tril4_s[:], in_=trilneg4),
        lambda s: s.dma_start(out=dtb_s[:], in_=dtb_rep),
        lambda s: s.dma_start(out=negA_s[:], in_=negA_rep),
        lambda s: s.dma_start(out=msend_s[:], in_=m_send),
        lambda s: s.dma_start(out=mrecv_s[:], in_=m_recv),
    ], deps={})
    step("G", [
        lambda g: g.memset(ones_colf[:], 1.0),
        lambda g: g.memset(ones_rowf[:], 1.0),
        lambda g: g.memset(ones_rowb[:], 1.0),
    ] + [
        (lambda i_, v_: lambda g: g.memset(cpool[:, i_:i_ + 1], v_))(i, v)
        for i, v in enumerate(CONSTS)
    ], deps={})

    def emit_ln_stats(xbuf, psR, psR2, sq0, sq1, strow):
        tprev = [0, 0]
        for k in range(NDB):
            sqb = sq0 if k % 2 == 0 else sq1
            sdep = {"DS": cnt["DS"], "DG": cnt["DG"], "V": cnt["V"], "T": tprev[k % 2]}
            step("S", [(lambda kk, sq_: lambda s: s.activation(sq_[:], xbuf[:, kk * TB:(kk + 1) * TB], AF.Square))(k, sqb)], deps=sdep)
            step("T", [
                (lambda kk, sq_: lambda t: t.matmul(psR2[0:1, 0:TB], ones_colf[:], sq_[:],
                                                    start=(kk == 0), stop=(kk == NDB - 1)))(k, sqb),
                (lambda kk: lambda t: t.matmul(psR[0:1, 0:TB], ones_colf[:], xbuf[:, kk * TB:(kk + 1) * TB],
                                               start=(kk == 0), stop=(kk == NDB - 1)))(k),
            ], deps={"S": cnt["S"], "G": cnt["G"]})
            tprev[k % 2] = cnt["T"]
        step("S", [
            lambda s: s.mul(strow[0:1, 0:TB], psR[0:1, 0:TB], 1.0 / D),
            lambda s: s.mul(strow[0:1, 4 * TB:5 * TB], psR2[0:1, 0:TB], 1.0 / D),
            lambda s: s.activation(strow[0:1, 5 * TB:6 * TB], strow[0:1, 0:TB], AF.Square),
        ])
        step("V", [
            lambda v: v.tensor_sub(strow[0:1, TB:2 * TB], strow[0:1, 4 * TB:5 * TB], strow[0:1, 5 * TB:6 * TB]),
        ])
        step("S", [
            lambda s: s.activation(strow[0:1, 6 * TB:7 * TB], strow[0:1, TB:2 * TB], AF.Ln, bias=EPS),
            lambda s: s.activation(strow[0:1, 2 * TB:3 * TB], strow[0:1, 6 * TB:7 * TB], AF.Exp, scale=-0.5),
        ])
        step("V", [
            lambda v: v.tensor_mul(strow[0:1, 7 * TB:8 * TB], strow[0:1, 0:TB], strow[0:1, 2 * TB:3 * TB]),
        ])
        step("S", [
            lambda s: s.mul(strow[0:1, 3 * TB:4 * TB], strow[0:1, 7 * TB:8 * TB], -1.0),
        ])

    def emit_ln_apply(xbuf, hbuf, psA, psB, sq0, strow):
        step("T", [
            lambda t: t.matmul(psA[:, 0:TB], ones_rowf[:], strow[0:1, 2 * TB:3 * TB], start=True, stop=True),
            lambda t: t.matmul(psB[:, 0:TB], ones_rowf[:], strow[0:1, 3 * TB:4 * TB], start=True, stop=True),
        ])
        fns = []
        for k in range(NDB):
            fns.append((lambda kk: lambda v: v.tensor_tensor(out=sq0[:], in0=xbuf[:, kk * TB:(kk + 1) * TB], in1=psA[:, 0:TB], op=OP.mult))(k))
            fns.append((lambda kk: lambda v: v.tensor_tensor(out=hbuf[:, kk * TB:(kk + 1) * TB], in0=sq0[:], in1=psB[:, 0:TB], op=OP.add))(k))
        step("V", fns)

    _ffnctr = [0]

    def emit_ffn_phase(resid_in, resid_out, ln2_out, use_wcache, write_wcache):
        sc = Scope()
        _ffnctr[0] += 1
        sfx = str(_ffnctr[0])
        xblk = sc.sb("xblk" + sfx, [128, NDB * TB], F32)
        h1blk = sc.sb("h1blk" + sfx, [128, NDB * TB], BF16)
        wslab = sc.sb("wslabF" + sfx, [128, 2 * NDB * 128], BF16)
        sq0 = sc.sb("sq0F" + sfx, [128, TB], F32)
        sq1 = sc.sb("sq1F" + sfx, [128, TB], F32)
        strow = sc.sb("strowF" + sfx, [1, 8 * TB], F32)
        lgz = sc.sb("lgz" + sfx, [128, 4 * 40], F32)
        ohb = sc.sb("ohb" + sfx, [128, 4 * 16], F32)
        zmb = sc.sb("zmb" + sfx, [128, 4 * 32], BF16)
        zmT = sc.sb("zmT" + sfx, [32, 4 * 128], BF16)
        psA = sc.ps("psA" + sfx, [128, TB])
        psB = sc.ps("psB" + sfx, [128, TB])
        psM = sc.ps("psM" + sfx, [128, TB])
        psM2 = sc.ps("psM2" + sfx, [128, TB])
        psT = sc.ps("psT" + sfx, [32, 128], BF16)
        psR = sc.ps("psR" + sfx, [1, TB])
        psR2 = sc.ps("psR2" + sfx, [1, TB])
        ntt = TB // 128
        for b in range(TO // TB):
            step("DS", [
                (lambda bb: lambda s: s.dma_start(
                    out=xblk[:].rearrange("p (k n) -> p k n", n=TB),
                    in_=resid_in.rearrange("(k p) n -> p k n", p=128)[:, :, bb * TB:(bb + 1) * TB]))(b),
            ])
            emit_ln_stats(xblk, psR, psR2, sq0, sq1, strow)
            emit_ln_apply(xblk, h1blk, psA, psB, sq0, strow)
            vh1 = cnt["V"]
            for tt in range(ntt):
                fns = []
                for k in range(NDB):
                    fns.append((lambda kk, t2: lambda t: t.matmul(
                        psM[:, t2 * 40:t2 * 40 + 36],
                        h1blk[:, kk * TB + t2 * 128:kk * TB + t2 * 128 + 128],
                        gwa_s[:, kk * 36:(kk + 1) * 36], start=(kk == 0), stop=False))(k, tt))
                fns.append((lambda t2: lambda t: t.matmul(
                    psM[:, t2 * 40:t2 * 40 + 4], ones_rowb[:], eb_s[:], start=False, stop=True))(tt))
                step("T", fns, deps={"V": vh1, "DG": cnt["DG"], "G": cnt["G"], "DS": cnt["DS"]})
            for tt in range(ntt):
                o, q = tt * 40, tt * 16
                step("V", [
                    (lambda o_: lambda v: v.tensor_copy(out=lgz[:, o_:o_ + 40], in_=psM[:, o_:o_ + 40]))(o),
                    (lambda o_, q_: lambda v: v.tensor_reduce(out=ohb[:, q_:q_ + 1], in_=lgz[:, o_:o_ + 4], axis=AX.X, op=OP.max))(o, q),
                ])
                step("S", [
                    (lambda q_: lambda s: s.mul(ohb[:, q_ + 6:q_ + 7], ohb[:, q_:q_ + 1], -65536.0))(q),
                ])
                step("S", [
                    (lambda o_, q_: lambda s: s.activation(
                        ohb[:, q_ + 1:q_ + 5], lgz[:, o_:o_ + 4], AF.Exp,
                        scale=65536.0, bias=ohb[:, q_ + 6:q_ + 7]))(o, q),
                ])
                step("V", [
                    (lambda o_, q_, t2: lambda v: v.tensor_tensor(
                        out=zmb[:, t2 * 32:(t2 + 1) * 32].rearrange("p (e r) -> p e r", e=4),
                        in0=lgz[:, o_ + 4:o_ + 36].rearrange("p (e r) -> p e r", e=4),
                        in1=ohb[:, q_ + 1:q_ + 5].unsqueeze(2).broadcast_to([128, 4, 8]),
                        op=OP.mult))(o, q, tt),
                ])
                step("T", [
                    (lambda t2: lambda t: t.transpose(psT[:, :], zmb[:, t2 * 32:(t2 + 1) * 32], idn_b[:]))(tt),
                ])
                step("V", [
                    (lambda t2: lambda v: v.tensor_copy(out=zmT[0:32, t2 * 128:(t2 + 1) * 128], in_=psT[:, :]))(tt),
                ])
            vzt = cnt["V"]
            T0 = cnt["T"]
            tmm_hist = {}
            vev_hist = {}
            for ot in range(NDB):
                buf = ot % 2
                wdep = {"T": tmm_hist.get(ot - 2, T0), "DS": cnt["DS"]}
                if use_wcache:
                    wl = step("DS", [
                        (lambda o2, b2: lambda s: s.dma_start(
                            out=wslab[:, b2 * NDB * 128:(b2 + 1) * NDB * 128].rearrange("p (k n) -> p k n", n=128),
                            in_=wupbf_d.rearrange("(k p) n -> p k n", p=128)[:, :, o2 * 128:(o2 + 1) * 128]))(ot, buf),
                    ], deps=wdep)
                    wk = "DS"
                else:
                    wl = step("DG", [
                        (lambda o2, b2: lambda g: g.dma_start(
                            out=wslab[:, b2 * NDB * 128:(b2 + 1) * NDB * 128].rearrange("p (k n) -> p k n", n=128),
                            in_=w_upT.rearrange("(k p) n -> p k n", p=128)[:, :, o2 * 128:(o2 + 1) * 128]))(ot, buf),
                    ], deps=wdep)
                    wk = "DG"
                    if write_wcache and b == 0:
                        step("DS", [
                            (lambda o2, b2: lambda s: s.dma_start(
                                out=wupbf_d.rearrange("(k p) n -> p k n", p=128)[:, :, o2 * 128:(o2 + 1) * 128],
                                in_=wslab[:, b2 * NDB * 128:(b2 + 1) * NDB * 128].rearrange("p (k n) -> p k n", n=128)))(ot, buf),
                        ], deps={wk: wl})
                pp = psM if ot % 2 == 0 else psM2
                fns = []
                for k in range(NDB):
                    fns.append((lambda kk, b2, pp_: lambda t: t.matmul(
                        pp_[:, 0:TB],
                        wslab[:, b2 * NDB * 128 + kk * 128:b2 * NDB * 128 + (kk + 1) * 128],
                        h1blk[:, kk * TB:(kk + 1) * TB], start=(kk == 0), stop=False))(k, buf, pp))
                for tt in range(ntt):
                    fns.append((lambda t2, o2, pp_: lambda t: t.matmul(
                        pp_[:, t2 * 128:(t2 + 1) * 128], ball_s[:, o2 * 128:(o2 + 1) * 128],
                        zmT[0:32, t2 * 128:(t2 + 1) * 128], start=False, stop=(t2 == ntt - 1)))(tt, ot, pp))
                tmm_hist[ot] = step("T", fns, deps={wk: wl, "V": max(vzt, vev_hist.get(ot - 2, 0))})
                vev_hist[ot] = step("V", [
                    (lambda o2, pp_: lambda v: v.scalar_tensor_tensor(
                        out=xblk[:, o2 * TB:(o2 + 1) * TB], in0=pp_[:, 0:TB], scalar=0.5,
                        in1=xblk[:, o2 * TB:(o2 + 1) * TB], op0=OP.mult, op1=OP.add))(ot, pp),
                ], deps={"T": tmm_hist[ot], "DS": cnt["DS"]})
                step("DS", [
                    (lambda o2, bb: lambda s: s.dma_start(
                        out=resid_out.rearrange("(k p) n -> k p n", p=128)[o2, :, bb * TB:(bb + 1) * TB],
                        in_=xblk[:, o2 * TB:(o2 + 1) * TB]))(ot, b),
                ], deps={"V": vev_hist[ot]})
            if ln2_out is not None:
                emit_ln_stats(xblk, psR, psR2, sq0, sq1, strow)
                emit_ln_apply(xblk, h1blk, psA, psB, sq0, strow)
                step("DS", [
                    (lambda bb: lambda s: s.dma_start(
                        out=ln2_out.rearrange("(k p) n -> p k n", p=128)[:, :, bb * TB:(bb + 1) * TB],
                        in_=h1blk[:].rearrange("p (k n) -> p k n", n=TB)))(b),
                ])
        sc.close()

    emit_ffn_phase(xT, x1T_d, h2T_d, use_wcache=False, write_wcache=True)

    sc1 = Scope()
    h2sup = sc1.sb("h2sup", [128, NDB * 1024], BF16)
    wslab = sc1.sb("wslab1", [128, 2 * NDB * 128], BF16)
    evst = sc1.sb("evst1", [128, 2 * TB], F32)
    evstb = sc1.sb("evstb1", [128, 2 * TB], BF16)
    psM = sc1.ps("p1M", [128, TB])
    psM2 = sc1.ps("p1M2", [128, TB])
    for ts in range(2):
        step("DS", [
            (lambda t2: lambda s: s.dma_start(
                out=h2sup[:].rearrange("p (k n) -> p k n", n=1024),
                in_=h2T_d.rearrange("(k p) n -> p k n", p=128)[:, :, t2 * 1024:(t2 + 1) * 1024]))(ts),
        ])
        ldh2 = cnt["DS"]
        T0 = cnt["T"]
        tmm_hist = {}
        vev_hist = {}
        for ot in range(PROJ // 128):
            buf = ot % 2
            wl = step("DG", [
                (lambda o2, b2: lambda g: g.dma_start(
                    out=wslab[:, b2 * NDB * 128:(b2 + 1) * NDB * 128].rearrange("p (k n) -> p k n", n=128),
                    in_=w_inT.rearrange("(k p) n -> p k n", p=128)[:, :, o2 * 128:(o2 + 1) * 128]))(ot, buf),
            ], deps={"T": tmm_hist.get(ot - 2, T0), "DS": ldh2})
            for nb in range(2):
                pp = psM if nb == 0 else psM2
                fns = []
                for k in range(NDB):
                    fns.append((lambda kk, b2, n2, pp_: lambda t: t.matmul(
                        pp_[:, 0:512],
                        wslab[:, b2 * NDB * 128 + kk * 128:b2 * NDB * 128 + (kk + 1) * 128],
                        h2sup[:, kk * 1024 + n2 * 512:kk * 1024 + n2 * 512 + 512],
                        start=(kk == 0), stop=(kk == NDB - 1)))(k, buf, nb, pp))
                tmm = step("T", fns, deps={"DG": wl, "DS": ldh2,
                                           "V": vev_hist.get((ot - 1, nb), 0)})
                tmm_hist[ot] = tmm
                col = ts * 1024 + nb * 512
                if ot < 64:
                    dst, stg = gateT_d.rearrange("(k p) n -> k p n", p=128)[ot], evstb[:, (nb % 2) * TB:(nb % 2) * TB + 512]
                elif ot < 144:
                    dst, stg = xBCT_d.rearrange("(k p) n -> k p n", p=128)[ot - 64], evstb[:, (nb % 2) * TB:(nb % 2) * TB + 512]
                else:
                    dst, stg = dtF_d, evst[:, (nb % 2) * TB:(nb % 2) * TB + 512]
                vev_hist[(ot, nb)] = step("V", [
                    (lambda pp_, st_: lambda v: v.tensor_copy(out=st_, in_=pp_[:, 0:512]))(pp, stg),
                ], deps={"T": tmm, "DS": cnt["DS"]})
                step("DS", [
                    (lambda d_, st_, c_: lambda s: s.dma_start(out=d_[:, c_:c_ + 512], in_=st_))(dst, stg, col),
                ], deps={"V": vev_hist[(ot, nb)]})
    sc1.close()

    scx = Scope()
    halo_s = scx.sb("halo_s", [128, 240], BF16)
    step("DS", [
        lambda s: s.dma_start(out=halo_s[:].rearrange("p (b t) -> p b t", t=3),
                              in_=xBCT_d.rearrange("(b p) n -> p b n", p=128)[:, :, TO - 3:TO],
                              allow_slow_non_contiguous=True),
    ])
    step("V", [
        lambda v: v.tensor_scalar(out=halo_s[:], in0=halo_s[:], scalar1=msend_s[:], scalar2=None, op0=OP.mult),
    ])
    step("DS", [lambda s: s.dma_start(out=cc1i, in_=halo_s[:])])
    step("G", [
        lambda g: g.collective_compute("AllReduce", OP.add, replica_groups=RG, ins=[cc1i], outs=[cc1o]),
    ])
    step("DS", [lambda s: s.dma_start(out=halo_s[:], in_=cc1o)])
    step("V", [
        lambda v: v.tensor_scalar(out=halo_s[:], in0=halo_s[:], scalar1=mrecv_s[:], scalar2=None, op0=OP.mult),
    ])

    sc2 = Scope()
    xs_tok = sc2.sb("xs_tok", [128, INTER], BF16)
    b_tok = sc2.sb("b_tok", [128, NG * 128], BF16)
    xdd = sc2.sb("xdd", [128, INTER], BF16)
    cs_all = sc2.sb("cs_all", [128, NCH * 128], F32)
    lndt_all = sc2.sb("lndt_all", [128, NCH * 128], F32)
    cslast_all = sc2.sb("cslast_all", [1, NCH * 128], F32)
    cdrow_all = sc2.sb("cdrow_all", [1, NCH * 128], F32)
    wTf = sc2.sb("wTf", [128, 128], F32)
    wTc = sc2.sb("wTc", [128, 128], BF16)
    cdmat = sc2.sb("cdmat", [128, 128], F32)
    psT2 = sc2.ps("psTc", [128, 128])
    psR3 = sc2.ps("psR3", [1, 128])
    psS1 = sc2.ps("psS1", [128, 512])
    psS2 = sc2.ps("psS2", [128, 512])
    sscr = sc2.sb("sscr", [128, 512], F32)

    def emit_dt_cs(c):
        step("DS", [
            (lambda c_: lambda s: s.dma_start(out=dtw[:, 0:128], in_=dtF_d[:, c_ * 128:(c_ + 1) * 128]))(c),
        ])
        step("T", [lambda t: t.transpose(psT2[:, :], dtw[:, 0:128], idn_f[:])])
        step("V", [lambda v: v.tensor_tensor(out=dtw[:, 128:256], in0=psT2[:, :], in1=dtb_s[:], op=OP.add)])
        step("S", [
            lambda s: s.activation(dtw[:, 256:384], dtw[:, 128:256], AF.Exp),
            lambda s: s.activation(dtw[:, 384:512], dtw[:, 256:384], AF.Ln, bias=1.0),
        ])
        step("V", [
            lambda v: v.tensor_scalar(out=dtw[:, 0:128], in0=dtw[:, 384:512], scalar1=100.0, scalar2=0.001, op0=OP.min, op1=OP.max),
        ])
        step("S", [
            (lambda c_: lambda s: s.activation(lndt_all[:, c_ * 128:(c_ + 1) * 128], dtw[:, 0:128], AF.Ln))(c),
        ])
        step("V", [
            lambda v: v.tensor_tensor(out=dtw[:, 128:256], in0=dtw[:, 0:128], in1=negA_s[:], op=OP.mult),
        ])
        step("T", [
            lambda t: t.matmul(psT2[:, :], LT1_s[:], dtw[:, 128:256], start=True, stop=True),
            lambda t: t.matmul(psR3[0:1, 0:128], ones_colf[:], dtw[:, 128:256], start=True, stop=True),
        ])
        step("V", [
            (lambda c_: lambda v: v.tensor_copy(out=cs_all[:, c_ * 128:(c_ + 1) * 128], in_=psT2[:, :]))(c),
            (lambda c_: lambda v: v.tensor_copy(out=cslast_all[0:1, c_ * 128:(c_ + 1) * 128], in_=psR3[0:1, 0:128]))(c),
        ])
        step("S", [
            (lambda c_: lambda s: s.activation(cdrow_all[0:1, c_ * 128:(c_ + 1) * 128], psR3[0:1, 0:128], AF.Exp))(c),
        ])

    def emit_wT_xdd(c):
        step("T", [
            (lambda c_: lambda t: t.matmul(psT2[:, :], ones_rowf[:], cslast_all[0:1, c_ * 128:(c_ + 1) * 128], start=True, stop=True))(c),
        ])
        step("V", [
            (lambda c_: lambda v: v.tensor_sub(wTf[:], lndt_all[:, c_ * 128:(c_ + 1) * 128], cs_all[:, c_ * 128:(c_ + 1) * 128]))(c),
            lambda v: v.tensor_tensor(out=wTf[:], in0=wTf[:], in1=psT2[:, :], op=OP.add),
        ])
        step("S", [lambda s: s.activation(wTc[:], wTf[:], AF.Exp)])
        wb = wTc[:].unsqueeze(2).broadcast_to([128, 128, 64])
        svw = cnt["S"]
        step("V", [
            (lambda wb_: lambda v: v.tensor_tensor(
                out=xdd[:, 0:4096].rearrange("p (h q) -> p h q", q=64),
                in0=xs_tok[:, 0:4096].rearrange("p (h q) -> p h q", q=64),
                in1=wb_[:, 0:64], op=OP.mult))(wb),
        ], deps={"S": svw, "DS": cnt["DS"], "G": cnt["G"], "T": cnt["T"]})
        step("G", [
            (lambda wb_: lambda g: g.tensor_tensor(
                out=xdd[:, 4096:8192].rearrange("p (h q) -> p h q", q=64),
                in0=xs_tok[:, 4096:8192].rearrange("p (h q) -> p h q", q=64),
                in1=wb_[:, 64:128], op=OP.mult))(wb),
        ], deps={"S": svw, "DS": cnt["DS"], "V": cnt["V"], "T": cnt["T"]})

    def emit_states_scan(c, prev):
        step("T", [
            (lambda c_: lambda t: t.matmul(psT2[:, :], ones_rowf[:], cdrow_all[0:1, c_ * 128:(c_ + 1) * 128], start=True, stop=True))(c),
        ])
        step("V", [lambda v: v.tensor_copy(out=cdmat[:], in_=psT2[:, :])])
        for gh in range(16):
            g_ = gh // 2
            pp = psS1 if gh % 2 == 0 else psS2
            tst = step("T", [
                (lambda g2, gh_, pp_: lambda t: t.matmul(
                    pp_[:, 0:512], b_tok[:, g2 * 128:(g2 + 1) * 128],
                    xdd[:, gh_ * 512:(gh_ + 1) * 512], start=True, stop=True))(g_, gh, pp),
            ])
            cdb = cdmat[:, gh * 8:(gh + 1) * 8].unsqueeze(2).broadcast_to([128, 8, 64])
            step("V", [
                (lambda gh_, cdb_, prev_: lambda v: v.tensor_tensor(
                    out=sscr[:].rearrange("p (h q) -> p h q", q=64),
                    in0=prev_[:, gh_ * 512:(gh_ + 1) * 512].rearrange("p (h q) -> p h q", q=64),
                    in1=cdb_, op=OP.mult))(gh, cdb, prev),
                (lambda gh_, pp_, prev_: lambda v: v.tensor_tensor(
                    out=prev_[:, gh_ * 512:(gh_ + 1) * 512], in0=sscr[:], in1=pp_[:, 0:512], op=OP.add))(gh, pp, prev),
            ], deps={"T": tst, "G": cnt["G"], "S": cnt["S"], "DS": cnt["DS"], "DG": cnt["DG"]})

    scA = Scope()
    win = scA.sb("win", [128, 80 * 131], BF16)
    dtw = scA.sb("dtw", [128, 4 * 128], F32)
    cvt = scA.sb("cvt", [128, 80 * 128], BF16)
    cvt2 = scA.sb("cvt2", [128, 80 * 128], BF16)
    prevA = scA.sb("prevA", [128, INTER], F32)
    step("G", [lambda g: g.memset(prevA[:], 0.0)])

    for c in range(NCH):
        emit_dt_cs(c)
        if c == 0:
            step("V", [
                lambda v: v.tensor_copy(out=win[:].rearrange("p (b t) -> p b t", t=131)[:, :, 0:3],
                                        in_=halo_s[:].rearrange("p (b t) -> p b t", t=3)),
            ])
            step("DS", [
                lambda s: s.dma_start(out=win[:].rearrange("p (b t) -> p b t", t=131)[:, :, 3:131],
                                      in_=xBCT_d.rearrange("(b p) n -> p b n", p=128)[:, :, 0:128]),
            ])
        else:
            step("DS", [
                (lambda c_: lambda s: s.dma_start(
                    out=win[:].rearrange("p (b t) -> p b t", t=131),
                    in_=xBCT_d.rearrange("(b p) n -> p b n", p=128)[:, :, c_ * 128 - 3:c_ * 128 + 128]))(c),
            ])
        wv = win[:].rearrange("p (b t) -> p b t", t=131)
        cw = lambda k: convw_s[:, k::4].unsqueeze(2).broadcast_to([128, 80, 128])
        c3 = lambda b_: b_[:].rearrange("p (b t) -> p b t", t=128)
        step("V", [
            (lambda wv_, cw_: lambda v: v.tensor_tensor(out=c3(cvt), in0=wv_[:, :, 0:128], in1=cw_, op=OP.mult))(wv, cw(0)),
        ])
        step("G", [
            (lambda wv_, cw_: lambda g: g.tensor_tensor(out=c3(cvt2), in0=wv_[:, :, 1:129], in1=cw_, op=OP.mult))(wv, cw(1)),
        ])
        step("V", [lambda v: v.tensor_tensor(out=cvt[:], in0=cvt[:], in1=cvt2[:], op=OP.add)])
        step("G", [
            (lambda wv_, cw_: lambda g: g.tensor_tensor(out=c3(cvt2), in0=wv_[:, :, 2:130], in1=cw_, op=OP.mult))(wv, cw(2)),
        ])
        step("V", [lambda v: v.tensor_tensor(out=cvt[:], in0=cvt[:], in1=cvt2[:], op=OP.add)])
        step("G", [
            (lambda wv_, cw_: lambda g: g.tensor_tensor(out=c3(cvt2), in0=wv_[:, :, 3:131], in1=cw_, op=OP.mult))(wv, cw(3)),
        ])
        step("V", [
            lambda v: v.tensor_tensor(out=cvt[:], in0=cvt[:], in1=cvt2[:], op=OP.add),
            lambda v: v.tensor_tensor(out=c3(cvt), in0=c3(cvt), in1=convb_s[:].unsqueeze(2).broadcast_to([128, 80, 128]), op=OP.add),
        ])
        step("S", [lambda s: s.activation(cvt2[:], cvt[:], AF.Silu)])
        step("DS", [
            lambda s: s.dma_start_transpose(xs_tok[:].rearrange("q (b p) -> q b p", p=128),
                                            cvt2[:, 0:64 * 128].rearrange("p (b q) -> p b q", q=128)),
            lambda s: s.dma_start_transpose(b_tok[:].rearrange("q (b p) -> q b p", p=128),
                                            cvt2[:, 64 * 128:72 * 128].rearrange("p (b q) -> p b q", q=128)),
        ])
        step("DS", [
            (lambda c_: lambda s: s.dma_start(out=xstok_d[c_ * 128:(c_ + 1) * 128, :], in_=xs_tok[:]))(c),
            (lambda c_: lambda s: s.dma_start(out=btok_d[c_ * 128:(c_ + 1) * 128, :], in_=b_tok[:]))(c),
            (lambda c_: lambda s: s.dma_start(
                out=bF_d.rearrange("(b p) n -> p b n", p=128)[:, :, c_ * 128:(c_ + 1) * 128],
                in_=cvt2[:, 64 * 128:72 * 128].rearrange("p (b q) -> p b q", q=128)))(c),
            (lambda c_: lambda s: s.dma_start(
                out=cF_d.rearrange("(b p) n -> p b n", p=128)[:, :, c_ * 128:(c_ + 1) * 128],
                in_=cvt2[:, 72 * 128:80 * 128].rearrange("p (b q) -> p b q", q=128)))(c),
        ], deps={"DS": cnt["DS"]})
        emit_wT_xdd(c)
        emit_states_scan(c, prevA)

    step("V", [
        lambda v: v.tensor_scalar(out=prevA[:], in0=prevA[:], scalar1=msend_s[:], scalar2=None, op0=OP.mult),
    ])
    step("DS", [lambda s: s.dma_start(out=cc2i, in_=prevA[:])])
    step("G", [
        lambda g: g.collective_compute("AllReduce", OP.add, replica_groups=RG, ins=[cc2i], outs=[cc2o]),
    ])
    scA.close()

    scB = Scope()
    prevT = scB.sb("prevT", [128, INTER], F32)
    prevTb = scB.sb("prevTb", [128, INTER], BF16)
    Drep_s = scB.sb("Drep_s", [128, INTER], BF16)
    gate_tok = scB.sb("gate_tok", [128, INTER], BF16)
    ybuf = scB.sb("ybuf", [128, INTER], BF16)
    normw_s = scB.sb("normw_s", [128, INTER], BF16)
    bF_c = scB.sb("bF_c", [128, NG * 128], BF16)
    cF_c = scB.sb("cF_c", [128, NG * 128], BF16)
    cbt_s = scB.sb("cbt_s", [128, NG * 128], BF16)
    lmat_s = scB.sb("lmat_s", [128, 128], BF16)
    scoT = scB.sb("scoT", [128, 2 * 128], BF16)
    expl_s = scB.sb("expl_s", [128, 128], F32)
    csF_s = scB.sb("csF_s", [128, 128], F32)
    csrp = scB.sb("csrp", [1, 512], F32)
    gnst = scB.sb("gnst", [128, 24], F32)
    psSeg = scB.ps("psSeg", [128, 512])
    psYo = scB.ps("psYo", [128, 512])
    psYd = scB.ps("psYd", [128, 512])

    step("DS", [lambda s: s.dma_start(out=prevT[:], in_=cc2o)])
    step("V", [
        lambda v: v.tensor_scalar(out=prevT[:], in0=prevT[:], scalar1=mrecv_s[:], scalar2=None, op0=OP.mult),
    ])
    if DEBUG[0]:
        step("DS", [lambda s: s.dma_start(out=dbg_prevT, in_=prevT[:]),
                    lambda s: s.dma_start(out=dbg_cs, in_=cs_all[:]),
                    lambda s: s.dma_start(out=dbg_lndt, in_=lndt_all[:]),
                    lambda s: s.dma_start(out=dbg_cd, in_=cdrow_all[:])])
    step("DG", [
        lambda g: g.dma_start(out=Drep_s[:], in_=Drep),
        lambda g: g.dma_start(out=normw_s[:], in_=normw_rep),
    ])

    for c in range(NCH):
        step("DS", [
            (lambda c_: lambda s: s.dma_start(out=xs_tok[:], in_=xstok_d[c_ * 128:(c_ + 1) * 128, :]))(c),
            (lambda c_: lambda s: s.dma_start(out=b_tok[:], in_=btok_d[c_ * 128:(c_ + 1) * 128, :]))(c),
            (lambda c_: lambda s: s.dma_start(
                out=bF_c[:].rearrange("p (b n) -> p b n", n=128),
                in_=bF_d.rearrange("(b p) n -> p b n", p=128)[:, :, c_ * 128:(c_ + 1) * 128]))(c),
            (lambda c_: lambda s: s.dma_start(
                out=cF_c[:].rearrange("p (b n) -> p b n", n=128),
                in_=cF_d.rearrange("(b p) n -> p b n", p=128)[:, :, c_ * 128:(c_ + 1) * 128]))(c),
            (lambda c_: lambda s: s.dma_start(
                out=ybuf[:].rearrange("p (b n) -> p b n", n=128),
                in_=gateT_d.rearrange("(b p) n -> p b n", p=128)[:, :, c_ * 128:(c_ + 1) * 128]))(c),
        ])
        step("DS", [
            lambda s: s.dma_start_transpose(gate_tok[:].rearrange("q (b p) -> q b p", p=128),
                                            ybuf[:].rearrange("p (b q) -> p b q", q=128)),
        ], deps={"DS": cnt["DS"]})
        step("S", [lambda s: s.activation(gate_tok[:], gate_tok[:], AF.Silu)])
        step("S", [lambda s: s.copy(out=prevTb[:], in_=prevT[:])])
        step("T", [
            (lambda c_: lambda t: t.transpose(psT2[:, :], cs_all[:, c_ * 128:(c_ + 1) * 128], idn_f[:]))(c),
        ])
        step("V", [lambda v: v.tensor_copy(out=csF_s[:], in_=psT2[:, :])])
        step("DS", [lambda s: s.dma_start(out=csf_d, in_=csF_s[:])])
        step("V", [
            (lambda c_: lambda v: v.tensor_sub(csF_s[:], lndt_all[:, c_ * 128:(c_ + 1) * 128], cs_all[:, c_ * 128:(c_ + 1) * 128]))(c),
        ])
        step("S", [
            (lambda c_: lambda s: s.activation(expl_s[:], cs_all[:, c_ * 128:(c_ + 1) * 128], AF.Exp))(c),
        ])
        for g_ in range(NG):
            step("T", [
                (lambda g2: lambda t: t.matmul(psT2[:, :], bF_c[:, g2 * 128:(g2 + 1) * 128],
                                               cF_c[:, g2 * 128:(g2 + 1) * 128], start=True, stop=True))(g_),
            ])
            step("S", [
                (lambda g2: lambda s: s.copy(out=cbt_s[:, g2 * 128:(g2 + 1) * 128], in_=psT2[:, :]))(g_),
            ])
        for gh in range(16):
            g_ = gh // 2
            tyo = step("T", [
                (lambda g2, gh_: lambda t: t.matmul(psYo[:, 0:512], cF_c[:, g2 * 128:(g2 + 1) * 128],
                                                    prevTb[:, gh_ * 512:(gh_ + 1) * 512], start=True, stop=True))(g_, gh),
            ])
            for q4 in range(2):
                ldrp = step("DS", [
                    (lambda gh_, q_: lambda s: s.dma_start(
                        out=csrp[:], in_=csf_d.rearrange("p n -> (p n)")[gh_ * 1024 + q_ * 512:gh_ * 1024 + (q_ + 1) * 512][None, :]))(gh, q4),
                ], deps={"T": cnt["T"]})
                tseg = step("T", [
                    lambda t: t.matmul(psSeg[:, 0:512], ones_rowf[:], csrp[0:1, 0:512], start=True, stop=False),
                    lambda t: t.matmul(psSeg[:, 0:512], idn_f[:], tril4_s[:], start=False, stop=True),
                ], deps={"DS": ldrp, "S": cnt["S"], "V": cnt["V"]})
                for hh in range(4):
                    h = gh * 8 + q4 * 4 + hh
                    sb_ = hh % 2
                    sexp = step("S", [
                        (lambda h_, hh_: lambda s: s.activation(
                            lmat_s[:], psSeg[:, hh_ * 128:(hh_ + 1) * 128], AF.Exp,
                            bias=csF_s[:, h_:h_ + 1]))(h, hh),
                    ], deps={"T": tseg, "V": cnt["V"]})
                    vsc = step("V", [
                        (lambda g2, s2: lambda v: v.tensor_tensor(
                            out=scoT[:, s2 * 128:(s2 + 1) * 128],
                            in0=cbt_s[:, g2 * 128:(g2 + 1) * 128], in1=lmat_s[:], op=OP.mult))(g_, sb_),
                    ], deps={"S": sexp, "T": cnt["T"]})
                    hsub = (q4 * 4 + hh) * 64
                    step("T", [
                        (lambda s2, hs_, h_: lambda t: t.matmul(
                            psYd[:, hs_:hs_ + 64], scoT[:, s2 * 128:(s2 + 1) * 128],
                            xs_tok[:, h_ * 64:(h_ + 1) * 64], start=True, stop=True))(sb_, hsub, h),
                    ], deps={"V": vsc})
            tyd = cnt["T"]
            ebr = expl_s[:, gh * 8:(gh + 1) * 8].unsqueeze(2).broadcast_to([128, 8, 64])
            step("V", [
                (lambda gh_, e_: lambda v: v.tensor_tensor(
                    out=ybuf[:, gh_ * 512:(gh_ + 1) * 512].rearrange("p (h q) -> p h q", q=64),
                    in0=psYo[:, 0:512].rearrange("p (h q) -> p h q", q=64), in1=e_, op=OP.mult))(gh, ebr),
                (lambda gh_: lambda v: v.tensor_tensor(
                    out=ybuf[:, gh_ * 512:(gh_ + 1) * 512], in0=ybuf[:, gh_ * 512:(gh_ + 1) * 512],
                    in1=psYd[:, 0:512], op=OP.add))(gh),
            ], deps={"T": tyd, "S": cnt["S"]})
        if DEBUG[0] and c == 0:
            step("DS", [lambda s: s.dma_start(out=dbg_y, in_=ybuf[:])])
        step("G", [
            lambda g: g.tensor_tensor(out=xdd[:], in0=xs_tok[:], in1=Drep_s[:], op=OP.mult),
        ])
        step("V", [
            lambda v: v.tensor_tensor(out=ybuf[:], in0=ybuf[:], in1=xdd[:], op=OP.add),
        ])
        step("G", [
            lambda g: g.tensor_tensor(out=ybuf[:], in0=ybuf[:], in1=gate_tok[:], op=OP.mult),
        ])
        if DEBUG[0] and c == 0:
            step("DS", [lambda s: s.dma_start(out=dbg_yg, in_=ybuf[:])])
        for g_ in range(NG):
            step("S", [
                (lambda g2: lambda s: s.activation(
                    xdd[:, 0:1024], ybuf[:, g2 * 1024:(g2 + 1) * 1024], AF.Square))(g_),
            ])
            step("V", [
                (lambda g2: lambda v: v.tensor_reduce(
                    out=gnst[:, g2:g2 + 1], in_=xdd[:, 0:1024], axis=AX.X, op=OP.add))(g_),
            ])
        step("S", [
            lambda s: s.activation(gnst[:, 8:16], gnst[:, 0:8], AF.Ln, bias=1024.0 * EPS),
            lambda s: s.activation(gnst[:, 16:24], gnst[:, 8:16], AF.Exp, scale=-0.5),
        ])
        if DEBUG[0] and c == 0:
            step("DS", [lambda s: s.dma_start(out=dbg_gnst, in_=gnst[:])])
        yn_fns = []
        for g_ in range(NG):
            yn_fns.append((lambda g2: lambda v: v.scalar_tensor_tensor(
                out=ybuf[:, g2 * 1024:(g2 + 1) * 1024], in0=ybuf[:, g2 * 1024:(g2 + 1) * 1024],
                scalar=gnst[:, 16 + g2:17 + g2], in1=normw_s[:, g2 * 1024:(g2 + 1) * 1024],
                op0=OP.mult, op1=OP.mult))(g_))
        step("V", yn_fns)
        step("DS", [
            lambda s: s.dma_start_transpose(gate_tok[:].rearrange("q (b p) -> q b p", p=128),
                                            ybuf[:, 0:64 * 128].rearrange("p (b q) -> p b q", q=128)),
        ])
        step("DS", [
            (lambda c_: lambda s: s.dma_start(
                out=ynT_d.rearrange("(b p) n -> p b n", p=128)[:, :, c_ * 128:(c_ + 1) * 128],
                in_=gate_tok[:].rearrange("p (b q) -> p b q", q=128)))(c),
        ], deps={"DS": cnt["DS"]})
        emit_wT_xdd(c)
        emit_states_scan(c, prevT)
    scB.close()
    sc2.close()
    scx.close()

    sc3 = Scope()
    ynblk = sc3.sb("ynblk", [128, 2 * 64 * TB], BF16)
    w3slab = sc3.sb("w3slab", [128, 2 * 64 * 128], BF16)
    x1ld = sc3.sb("x1ld", [128, 2 * TB], F32)
    evst = sc3.sb("evst3", [128, 2 * TB], F32)
    psM3 = sc3.ps("psM3", [128, TB])
    psM4 = sc3.ps("psM4", [128, TB])
    tmm_hist = {}
    vev_hist = {}
    ld_hist = {}
    NB3 = TO // TB
    for ot in range(NDB):
        buf = ot % 2
        wl = step("DG", [
            (lambda o2, b2: lambda g: g.dma_start(
                out=w3slab[:, b2 * 64 * 128:(b2 + 1) * 64 * 128].rearrange("p (k n) -> p k n", n=128),
                in_=w_outT.rearrange("(k p) n -> p k n", p=128)[:, :, o2 * 128:(o2 + 1) * 128]))(ot, buf),
        ], deps={"T": tmm_hist.get(ot - 2, 0)} if ot > 1 else all_deps("DG"))
        for nb in range(NB3):
            i = ot * NB3 + nb
            bb = i % 2
            ld_hist[i] = step("DS", [
                (lambda n2, b2: lambda s: s.dma_start(
                    out=ynblk[:, b2 * 64 * TB:(b2 + 1) * 64 * TB].rearrange("p (k n) -> p k n", n=TB),
                    in_=ynT_d.rearrange("(k p) n -> p k n", p=128)[:, :, n2 * TB:(n2 + 1) * TB]))(nb, bb),
                (lambda o2, n2, b2: lambda s: s.dma_start(
                    out=x1ld[:, b2 * TB:(b2 + 1) * TB],
                    in_=x1T_d.rearrange("(k p) n -> k p n", p=128)[o2, :, n2 * TB:(n2 + 1) * TB]))(ot, nb, bb),
            ], deps={"T": tmm_hist.get(i - 2, 0), "V": vev_hist.get(i - 2, 0)} if i > 1 else all_deps("DS"))
            pp = psM3 if i % 2 == 0 else psM4
            fns = []
            for k in range(64):
                fns.append((lambda kk, b2, bb2, pp_: lambda t: t.matmul(
                    pp_[:, 0:TB],
                    w3slab[:, b2 * 64 * 128 + kk * 128:b2 * 64 * 128 + (kk + 1) * 128],
                    ynblk[:, bb2 * 64 * TB + kk * TB:bb2 * 64 * TB + (kk + 1) * TB],
                    start=(kk == 0), stop=(kk == 63)))(k, buf, bb, pp))
            tmm_hist[i] = step("T", fns, deps={"DG": wl, "DS": ld_hist[i], "V": vev_hist.get(i - 1, 0)})
            vev_hist[i] = step("V", [
                (lambda pp_, b2: lambda v: v.tensor_tensor(out=evst[:, b2 * TB:(b2 + 1) * TB],
                                                           in0=pp_[:, 0:TB], in1=x1ld[:, b2 * TB:(b2 + 1) * TB], op=OP.add))(pp, bb),
            ], deps={"T": tmm_hist[i], "DS": cnt["DS"]})
            step("DS", [
                (lambda o2, n2, b2: lambda s: s.dma_start(
                    out=x2T_d.rearrange("(k p) n -> k p n", p=128)[o2, :, n2 * TB:(n2 + 1) * TB],
                    in_=evst[:, b2 * TB:(b2 + 1) * TB]))(ot, nb, bb),
            ], deps={"V": vev_hist[i]})
    sc3.close()

    emit_ffn_phase(x2T_d, xoutT, None, use_wcache=True, write_wcache=False)

    fin = all_deps()

    def final_wait(eng):
        for k, v in fin.items():
            if k != "DS" and v > 0:
                eng.wait_ge(sems[k], v)
        eng.wait_ge(sems["DS"], cnt["DS"])
    prog.append(("DS", final_wait))

    def runner(keys):
        fns = [f for k, f in prog if k in keys]

        def run(eng):
            for f in fns:
                f(eng)
        return run

    block.tensor(runner({"T"}))
    block.vector(runner({"V"}))
    block.scalar(runner({"S"}))
    block.gpsimd(runner({"G", "DG"}))
    block.sync(runner({"DS"}))

    blk_mgr.__exit__(None, None, None)
    gsc.close()
    for m in reversed(open_mgrs):
        m.__exit__(None, None, None)
    return nc


_NC_CACHE = None


def kernel(**inputs):
    global _NC_CACHE
    xs = np.asarray(inputs["L0_attn_hidden_states"])
    w_in = np.asarray(inputs["L0_attn_in_proj_w"])
    conv_w = np.asarray(inputs["L0_attn_conv_w"])
    conv_b = np.asarray(inputs["L0_attn_conv_b"])
    dt_bias = np.asarray(inputs["L0_attn_dt_bias"])
    Dv = np.asarray(inputs["L0_attn_D"])
    norm_w = np.asarray(inputs["L0_attn_norm_w"])
    w_out = np.asarray(inputs["L0_attn_out_proj_w"])
    w_up = np.asarray(inputs["L0_ffn_W_up"])
    gate_w = np.asarray(inputs["L0_ffn_gate_w"])
    ebias = np.asarray(inputs["L0_ffn_expert_bias"])
    lora_A = np.asarray(inputs["L0_ffn_lora_A"])
    lora_B = np.asarray(inputs["L0_ffn_lora_B"])

    if _NC_CACHE is None:
        _NC_CACHE = build_program()
    nc = _NC_CACHE

    w_upT = np.ascontiguousarray(w_up.T)
    A_all = lora_A.reshape(32, 4096)
    gwa_T = np.ascontiguousarray(np.concatenate([gate_w, A_all], 0).T)
    b_all = np.ascontiguousarray(lora_B.transpose(0, 2, 1).reshape(32, 4096))
    w_inT = np.ascontiguousarray(w_in.T)
    w_outT = np.ascontiguousarray(w_out.T)
    cw = conv_w[:, 0, :]
    convw = np.ascontiguousarray(cw.reshape(80, 128, 4).transpose(1, 0, 2).reshape(128, 320))
    convb = np.ascontiguousarray(conv_b.reshape(80, 128).T)
    dtb_rep = np.ascontiguousarray(np.broadcast_to(dt_bias[None, :], (128, 128)))
    negA_rep = np.ascontiguousarray(np.broadcast_to(
        (-np.arange(1, 129, dtype=np.float32))[None, :], (128, 128)))
    Drep = np.ascontiguousarray(np.broadcast_to(
        np.repeat(Dv, 64)[None, :], (128, 8192)))
    normw_rep = np.ascontiguousarray(np.broadcast_to(
        (norm_w * 32.0)[None, :], (128, 8192))).astype(np.float32)
    LT1 = np.ascontiguousarray(np.tril(np.ones((128, 128), np.float32)).T)
    tril1 = np.where(np.arange(128)[:, None] > np.arange(128)[None, :], NEG, 0.0).astype(np.float32)
    trilneg4 = np.ascontiguousarray(np.tile(tril1, (1, 4)))
    idn = np.eye(128, dtype=np.float32)
    eb_row = np.ascontiguousarray(ebias.reshape(1, 4))

    in_maps = []
    for core in range(8):
        b, half = core // 2, core % 2
        xTc = np.ascontiguousarray(xs[b, half * TO:(half + 1) * TO, :].T)
        m_send = np.full((128, 1), 1.0 - half, np.float32)
        m_recv = np.full((128, 1), float(half), np.float32)
        in_maps.append({
            "xT": xTc, "w_upT": w_upT, "gwa_T": gwa_T, "eb_row": eb_row, "b_all": b_all,
            "w_inT": w_inT, "convw": convw, "convb": convb, "dtb_rep": dtb_rep,
            "negA_rep": negA_rep, "Drep": Drep, "normw_rep": normw_rep, "w_outT": w_outT,
            "LT1": LT1, "trilneg4": trilneg4, "idn": idn, "m_send": m_send, "m_recv": m_recv,
        })
    res = run_bass_kernel_spmd(nc, in_maps, core_ids=list(range(8)))
    out = np.empty((4, 4096, 4096), np.float32)
    for core in range(8):
        b, half = core // 2, core % 2
        out[b, half * TO:(half + 1) * TO, :] = np.asarray(res.results[core]["xoutT"]).T
    if DEBUG[0]:
        return out, res
    return out


if __name__ == "__main__":
    import reference as R
    ins = {k: np.asarray(v) for k, v in R.setup_inputs().items()}
    out = kernel(**ins)
    ref = np.asarray(R.reference(**ins))
    num = float(((out - ref) ** 2).mean())
    den = float((ref ** 2).mean())
    print("resvar", num / den)
